# version 01
# baseline (speedup 1.0000x reference)
import numpy as np
from contextlib import ExitStack

import concourse.bass as bass
import concourse.mybir as mybir
from concourse.bass_utils import run_bass_kernel_spmd

F32 = mybir.dt.float32
BF16 = mybir.dt.bfloat16
AF = mybir.ActivationFunctionType
ALU = mybir.AluOpType
AX = mybir.AxisListType.X

D = 1024
S = 2048
NT = 16
TB = 512
NB = 4
NSUB = 4
E = 12
HID = 4096
NH = 16
HD = 64
KC = 8

AR32_N = 24400
AR16_N = 57472

ENGINES = ("tensor", "scalar", "vector", "sync", "gpsimd")

FULL_CHAIN = ("moe", "attn0", "moe", "l1ffn", "attn1", "l1ffn")


class Sem:
    def __init__(self, es, nc, name, step):
        self.h = es.enter_context(nc.semaphore(name))
        self.step = step
        self.n = 0


class Alloc:

    def __init__(self, tensor, n):
        self.t = tensor
        self.n = n
        self.off = 0
        self.high = 0

    def alloc(self, cols, rows=128, align=256):
        self.off = (self.off + align - 1) // align * align
        a = self.off
        self.off += cols
        self.high = max(self.high, self.off)
        assert self.off <= self.n, f"arena overflow: {self.off} > {self.n}"
        return self.t[0:rows, a:a + cols]

    def mark(self):
        return self.off

    def release(self, m):
        self.off = m


class Ctx:

    def __init__(self, nc, es):
        self.nc = nc
        self.es = es
        self.q = {e: [] for e in ENGINES}
        self.sem = {}
        for name, step in (("sW", 16), ("sG", 16), ("sT", 1), ("sA", 1),
                           ("sV", 1), ("sO", 16)):
            self.sem[name] = Sem(es, nc, name, step)

    def op(self, qn, f, inc=None):
        if inc is None:
            self.q[qn].append(lambda e, f=f: f(e))
            return None
        s = self.sem[inc]
        s.n += s.step
        self.q[qn].append(lambda e, f=f, s=s: f(e).then_inc(s.h, s.step))
        return s.n

    def wait(self, qn, semname, val):
        if val is None or val <= 0:
            return
        s = self.sem[semname]
        self.q[qn].append(lambda e, s=s, v=val: e.wait_ge(s.h, v))

    def barrier_vals(self):
        return {k: s.n for k, s in self.sem.items()}

    def wait_all(self, qn, vals, names=("sT", "sA", "sV", "sW", "sG")):
        for n in names:
            self.wait(qn, n, vals[n])

    def phase_barrier(self, queues=ENGINES, names=("sT", "sA", "sV", "sW", "sG")):
        vals = self.barrier_vals()
        for qn in queues:
            self.wait_all(qn, vals, names)
        return vals

    def tap(self, name, ap):
        if name not in getattr(self, "want_taps", ()):
            return
        o = self.nc.dram_tensor("tap_" + name, list(ap.shape), ap.dtype,
                                kind="ExternalOutput").ap()
        vals = self.barrier_vals()
        self.wait_all("sync", vals)
        self.op("sync", lambda e, oo=o, i=ap: e.dma_start(out=oo, in_=i), inc="sO")



def host_layout(inputs):
    ins = {}
    c = np.ascontiguousarray
    ins["x"] = np.asarray(inputs["L0_attn_x"], np.float32)
    wup = np.asarray(inputs["L0_ffn_Wup"], np.float32)
    ins["wup_blk"] = c(wup.reshape(E, 8, 128, 32, 128)
                       .transpose(0, 3, 2, 1, 4).reshape(E, 32, 128, 1024))
    ins["wdn"] = c(np.asarray(inputs["L0_ffn_Wdown"], np.float32))
    ins["wg"] = c(np.asarray(inputs["L0_ffn_Wg"], np.float32))
    ins["bup"] = c(np.asarray(inputs["L0_ffn_bup"], np.float32))
    ins["bdn"] = c(np.asarray(inputs["L0_ffn_bdown"], np.float32))
    for k in ("Wq", "Wk", "Wv", "Wo"):
        ins["l0" + k.lower()] = c(np.asarray(inputs["L0_attn_" + k], np.float32))
    ins["l0bo"] = c(np.asarray(inputs["L0_attn_bo"], np.float32).reshape(1, D))
    for k in ("Wq", "Wk", "Wv", "Wg", "Wo"):
        ins["l1" + k.lower()] = c(np.asarray(inputs["L1_attn_" + k], np.float32))
    for k in ("bq", "bk", "bv", "bg", "bo"):
        ins["l1" + k] = c(np.asarray(inputs["L1_attn_" + k], np.float32).reshape(1, D))
    ins["l1bq8"] = c(np.asarray(inputs["L1_attn_bq"], np.float32).reshape(8, 128))
    ins["l1bg8"] = c(np.asarray(inputs["L1_attn_bg"], np.float32).reshape(8, 128))

    def blkT(w, oc, kc):
        return c(np.asarray(w, np.float32).reshape(oc, 128, kc, 128)
                 .transpose(0, 3, 2, 1).reshape(oc, 128, kc * 128))
    ins["w1t"] = blkT(inputs["L1_ffn_w1_w"], 32, 8)
    ins["w2t"] = blkT(inputs["L1_ffn_w2_w"], 32, 8)
    ins["h1t"] = blkT(inputs["L1_ffn_h1_w"], 32, 32)
    ins["h2t"] = blkT(inputs["L1_ffn_h2_w"], 32, 32)
    ins["w3t"] = blkT(inputs["L1_ffn_w3_w"], 8, 32)
    ins["s1"] = c(np.asarray(inputs["L1_ffn_w1_s"], np.float32).reshape(32, 128))
    ins["s2"] = c(np.asarray(inputs["L1_ffn_w2_s"], np.float32).reshape(32, 128))
    ins["h1s"] = c(np.asarray(inputs["L1_ffn_h1_s"], np.float32).reshape(32, 128))
    ins["h2s"] = c(np.asarray(inputs["L1_ffn_h2_s"], np.float32).reshape(32, 128))
    ins["w3s"] = c(np.asarray(inputs["L1_ffn_w3_s"], np.float32).reshape(8, 128))
    kon = np.zeros((128, 8), np.float32)
    kon[:, 0] = 0.125
    kon[:, 1] = -0.5
    kon[:, 2] = 1e-5
    kon[:, 3] = 1.0
    kon[:, 4] = 0.5
    kon[:, 5] = float(np.log(0.125 * 1.05))
    ins["konst"] = kon
    ins["ident"] = np.eye(128, dtype=np.float32)
    ins["ones_row"] = np.ones((1, 128), np.float32)
    ins["ones_col"] = np.ones((128, 1), np.float32)
    hs = np.zeros((128, 128), np.float32)
    for ch in range(8):
        hs[0:64, ch * 16 + 2 * ch] = 1.0
        hs[64:128, ch * 16 + 2 * ch + 1] = 1.0
    ins["headsel"] = hs
    return ins



def declare_io(nc):
    d = {}
    dt = nc.dram_tensor
    d["x"] = dt("x", [S, D], F32, kind="ExternalInput").ap()
    d["out"] = dt("out", [S, D], F32, kind="ExternalOutput").ap()
    d["wup_blk"] = dt("wup_blk", [E, 32, 128, 1024], F32, kind="ExternalInput").ap()
    d["wdn"] = dt("wdn", [E, HID, D], F32, kind="ExternalInput").ap()
    d["wg"] = dt("wg", [D, E], F32, kind="ExternalInput").ap()
    d["bup"] = dt("bup", [E, HID], F32, kind="ExternalInput").ap()
    d["bdn"] = dt("bdn", [E, D], F32, kind="ExternalInput").ap()
    for k in ("l0wq", "l0wk", "l0wv", "l0wo"):
        d[k] = dt(k, [D, D], F32, kind="ExternalInput").ap()
    d["l0bo"] = dt("l0bo", [1, D], F32, kind="ExternalInput").ap()
    for k in ("l1wq", "l1wk", "l1wv", "l1wg", "l1wo"):
        d[k] = dt(k, [D, D], F32, kind="ExternalInput").ap()
    for k in ("l1bq", "l1bk", "l1bv", "l1bg", "l1bo"):
        d[k] = dt(k, [1, D], F32, kind="ExternalInput").ap()
    d["l1bq8"] = dt("l1bq8", [8, 128], F32, kind="ExternalInput").ap()
    d["l1bg8"] = dt("l1bg8", [8, 128], F32, kind="ExternalInput").ap()
    d["w1t"] = dt("w1t", [32, 128, 1024], F32, kind="ExternalInput").ap()
    d["w2t"] = dt("w2t", [32, 128, 1024], F32, kind="ExternalInput").ap()
    d["h1t"] = dt("h1t", [32, 128, 4096], F32, kind="ExternalInput").ap()
    d["h2t"] = dt("h2t", [32, 128, 4096], F32, kind="ExternalInput").ap()
    d["w3t"] = dt("w3t", [8, 128, 4096], F32, kind="ExternalInput").ap()
    for k in ("s1", "s2", "h1s", "h2s"):
        d[k] = dt(k, [32, 128], F32, kind="ExternalInput").ap()
    d["w3s"] = dt("w3s", [8, 128], F32, kind="ExternalInput").ap()
    d["konst"] = dt("konst", [128, 8], F32, kind="ExternalInput").ap()
    d["ident"] = dt("ident", [128, 128], F32, kind="ExternalInput").ap()
    d["ones_row"] = dt("ones_row", [1, 128], F32, kind="ExternalInput").ap()
    d["ones_col"] = dt("ones_col", [128, 1], F32, kind="ExternalInput").ap()
    d["headsel"] = dt("headsel", [128, 128], F32, kind="ExternalInput").ap()
    d["s_wup"] = dt("s_wup", [E, 32, 128, 1024], BF16).ap()
    d["s_wdn"] = dt("s_wdn", [E, HID, D], BF16).ap()
    d["s_w1t"] = dt("s_w1t", [32, 128, 1024], BF16).ap()
    d["s_w2t"] = dt("s_w2t", [32, 128, 1024], BF16).ap()
    d["s_h1t"] = dt("s_h1t", [32, 128, 4096], BF16).ap()
    d["s_h2t"] = dt("s_h2t", [32, 128, 4096], BF16).ap()
    d["s_w3t"] = dt("s_w3t", [8, 128, 4096], BF16).ap()
    return d


def build_nc(chain=FULL_CHAIN, want_taps=()):
    nc = bass.Bass("TRN2", target_bir_lowering=False)
    io = declare_io(nc)
    es = ExitStack()
    C = Ctx(nc, es)
    C.want_taps = want_taps
    C.io = io

    ar32_t = es.enter_context(nc.sbuf_tensor("ar32", [128, AR32_N], F32))
    ar16_t = es.enter_context(nc.sbuf_tensor("ar16", [128, AR16_N], BF16))
    C.a32 = Alloc(ar32_t, AR32_N)
    C.a16 = Alloc(ar16_t, AR16_N)
    C.psu = es.enter_context(nc.psum_tensor("psu", [128, 2048], F32))
    C.psd = es.enter_context(nc.psum_tensor("psd", [128, 2048], F32))

    C.R = C.a32.alloc(NT * D)
    C.SQ = C.a32.alloc(2 * D)
    C.XT = C.SQ
    C.MU = C.a32.alloc(NT, align=64)
    C.VARB = C.a32.alloc(NT, align=64)
    C.LNB = C.a32.alloc(NT, align=64)
    C.RS = C.a32.alloc(NT, align=64)
    C.NMR = C.a32.alloc(NT, align=64)
    C.SC1 = C.a32.alloc(1, align=4)
    C.SC2 = C.a32.alloc(1, align=4)
    C.SC3 = C.a32.alloc(1, align=4)
    C.ID = C.a32.alloc(128)
    C.KON = C.a32.alloc(8, align=32)
    C.ONES_R = C.a16.alloc(128, rows=1)
    C.ONES_C = C.a16.alloc(1, rows=128, align=64)
    C.HSEL = C.a16.alloc(128)
    C.S1 = C.a32.alloc(32, align=64)
    C.S2 = C.a32.alloc(32, align=64)
    C.H1S = C.a32.alloc(32, align=64)
    C.H2S = C.a32.alloc(32, align=64)
    C.W3S = C.a32.alloc(8, align=64)
    C.BQT = C.a32.alloc(8, align=64)
    C.BGT = C.a32.alloc(8, align=64)
    C.BUP = C.a32.alloc(32 * E, align=64)

    emit_prologue(C)

    for li, sub in enumerate(chain):
        C.phase_barrier()
        emit_ln(C)
        if sub == "moe":
            emit_moe(C, first=("moe" not in chain[:li]))
        elif sub == "attn0":
            emit_attn0(C)
        elif sub == "l1ffn":
            emit_l1ffn(C, first=("l1ffn" not in chain[:li]))
        elif sub == "attn1":
            emit_attn1(C)
        else:
            raise ValueError(sub)

    emit_epilogue(C)

    with es:
        with nc.Block() as block:
            block.tensor(lambda e: [f(e) for f in C.q["tensor"]])
            block.scalar(lambda e: [f(e) for f in C.q["scalar"]])
            block.vector(lambda e: [f(e) for f in C.q["vector"]])
            block.sync(lambda e: [f(e) for f in C.q["sync"]])
            block.gpsimd(lambda e: [f(e) for f in C.q["gpsimd"]])
    return nc



def emit_prologue(C):
    io = C.io
    xv = io["x"].rearrange("(t p) d -> p t d", p=128)
    rv = C.R.rearrange("p (t d) -> p t d", t=NT)
    C.op("sync", lambda e, o=rv, i=xv: e.dma_start(out=o, in_=i), inc="sW")
    C.op("sync", lambda e, o=C.ID, i=io["ident"]: e.dma_start(out=o, in_=i), inc="sW")
    C.op("sync", lambda e, o=C.KON, i=io["konst"]: e.dma_start(out=o, in_=i), inc="sW")
    C.op("gpsimd", lambda e, o=C.ONES_R, i=io["ones_row"]: e.dma_start(out=o, in_=i), inc="sG")
    C.op("gpsimd", lambda e, o=C.ONES_C, i=io["ones_col"]: e.dma_start(out=o, in_=i), inc="sG")
    C.op("gpsimd", lambda e, o=C.HSEL, i=io["headsel"]: e.dma_start(out=o, in_=i), inc="sG")
    m32 = C.a32.mark()
    bups = C.a32.alloc(HID, rows=12)
    wld = C.op("sync", lambda e, o=bups, i=io["bup"]: e.dma_start(out=o, in_=i), inc="sW")
    gld = C.sem["sG"].n
    C.wait("tensor", "sW", wld)
    C.wait("tensor", "sG", gld)
    prev = None
    for hc in range(32):
        tv = C.op("tensor", lambda e, o=C.psu[:, 0:12],
                  i=bups[:, hc * 128:(hc + 1) * 128], idm=C.ID[0:12, 0:12]:
                  e.transpose(o, i, idm), inc="sT")
        C.wait("vector", "sT", tv)
        prev = C.op("vector", lambda e, o=C.BUP[:, hc * E:(hc + 1) * E],
                    i=C.psu[:, 0:12]: e.tensor_copy(out=o, in_=i), inc="sV")
        C.wait("tensor", "sV", prev)
    stg = C.a32.alloc(128, rows=32)
    for name, dst, rows, scl in (("s1", C.S1, 32, None), ("s2", C.S2, 32, None),
                                 ("h1s", C.H1S, 32, None), ("h2s", C.H2S, 32, None),
                                 ("w3s", C.W3S, 8, 0.5), ("l1bq8", C.BQT, 8, None),
                                 ("l1bg8", C.BGT, 8, None)):
        wv = C.op("sync", lambda e, o=stg[0:rows, :], i=io[name]:
                  e.dma_start(out=o, in_=i), inc="sW")
        C.wait("tensor", "sW", wv)
        C.wait("tensor", "sV", C.sem["sV"].n)
        tv = C.op("tensor", lambda e, o=C.psu[:, 0:rows], i=stg[0:rows, :],
                  idm=C.ID[0:rows, 0:rows]: e.transpose(o, i, idm), inc="sT")
        C.wait("vector", "sT", tv)
        if scl is None:
            C.op("vector", lambda e, o=dst, i=C.psu[:, 0:rows]:
                 e.tensor_copy(out=o, in_=i), inc="sV")
        else:
            C.op("vector", lambda e, o=dst, i=C.psu[:, 0:rows], s=scl:
                 e.tensor_scalar(out=o, in0=i, scalar1=s, scalar2=None,
                                 op0=ALU.mult), inc="sV")
    C.a32.release(m32)
    C.const_barrier = C.phase_barrier(queues=("tensor", "scalar", "vector"))
    for ee in range(2):
        _emit_moe_cast(C, ee)
    C.pending_casts = ([("moe", ee) for ee in range(2, E)] +
                       [("gen", "w1t", "s_w1t"), ("gen", "w2t", "s_w2t"),
                        ("gen", "h1t", "s_h1t"), ("gen", "h2t", "s_h2t"),
                        ("gen", "w3t", "s_w3t")])
    C.l1_cast_done = 0


def _emit_moe_cast(C, ee):
    io = C.io
    C.op("gpsimd", lambda e, o=io["s_wup"][ee], i=io["wup_blk"][ee]:
         e.dma_start(out=o, in_=i), inc="sG")
    v = C.op("gpsimd", lambda e, o=io["s_wdn"][ee], i=io["wdn"][ee]:
             e.dma_start(out=o, in_=i), inc="sG")
    if not hasattr(C, "cast_done"):
        C.cast_done = {}
    C.cast_done[ee] = v


def _drain_pending_cast(C):
    if not C.pending_casts:
        return
    item = C.pending_casts.pop(0)
    if item[0] == "moe":
        _emit_moe_cast(C, item[1])
    else:
        _, src, dst = item
        io = C.io
        n = io[src].shape[0]
        h = max(1, n // 4)
        for a in range(0, n, h):
            b = min(n, a + h)
            C.l1_cast_done = C.op(
                "gpsimd", lambda e, o=io[dst][a:b], i=io[src][a:b]:
                e.dma_start(out=o, in_=i), inc="sG")



def emit_ln(C):
    sq_done = {}
    for t in range(NT):
        rt = C.R[:, t * D:(t + 1) * D]
        sq = C.SQ[:, (t % 2) * D:(t % 2) * D + D]
        if t >= 2:
            C.wait("scalar", "sV", sq_done[t - 2])
        av = C.op("scalar", lambda e, o=sq, i=rt:
                  e.activation(out=o, in_=i, func=AF.Square), inc="sA")
        v1 = C.op("vector", lambda e, o=C.SC1, i=rt: e.reduce_sum(o, i, AX), inc="sV")
        C.wait("vector", "sV", v1)
        mu = C.MU[:, t:t + 1]
        v2 = C.op("vector", lambda e, o=mu, i=C.SC1:
                  e.tensor_scalar(out=o, in0=i, scalar1=1.0 / D, scalar2=None,
                                  op0=ALU.mult), inc="sV")
        C.wait("vector", "sA", av)
        sq_done[t] = C.op("vector", lambda e, o=C.SC2, i=sq:
                          e.reduce_sum(o, i, AX), inc="sV")
        C.wait("vector", "sV", v2)
        v4 = C.op("vector", lambda e, o=C.SC3, a=mu, b=mu:
                  e.tensor_mul(o, a, b), inc="sV")
        C.wait("vector", "sV", v4)
        var = C.VARB[:, t:t + 1]
        v5 = C.op("vector", lambda e, o=var, i=C.SC2, m=C.SC3:
                  e.scalar_tensor_tensor(out=o, in0=i, scalar=1.0 / D, in1=m,
                                         op0=ALU.mult, op1=ALU.subtract), inc="sV")
        C.wait("scalar", "sV", v5)
        a2 = C.op("scalar", lambda e, o=C.LNB[:, t:t + 1], i=var, bi=C.KON[:, 2:3]:
                  e.activation(out=o, in_=i, func=AF.Ln, bias=bi), inc="sA")
        C.wait("scalar", "sA", a2)
        rs = C.RS[:, t:t + 1]
        a3 = C.op("scalar", lambda e, o=rs, i=C.LNB[:, t:t + 1], sc=C.KON[:, 1:2]:
                  e.activation(out=o, in_=i, func=AF.Exp, scale=sc), inc="sA")
        C.wait("vector", "sA", a3)
        C.op("vector", lambda e, o=C.NMR[:, t:t + 1], a=mu, b=rs:
             e.scalar_tensor_tensor(out=o, in0=a, scalar=-1.0, in1=b,
                                    op0=ALU.mult, op1=ALU.mult), inc="sV")
    C.ln_done = C.barrier_vals()
    C.tap("mu", C.MU)
    C.tap("rs", C.RS)
    C.tap("varb", C.VARB)
    C.tap("lnb", C.LNB)



def emit_xhat(C, b, XH):
    pre = C.barrier_vals()
    C.wait("scalar", "sV", C.ln_done["sV"])
    C.wait("tensor", "sV", pre["sV"])
    C.wait("tensor", "sA", pre["sA"])
    tr_done = {}
    cp_done = {}
    for s in range(NSUB):
        t = b * NSUB + s
        xt = C.XT[:, (s % 2) * D:(s % 2) * D + D]
        if s >= 2:
            C.wait("scalar", "sT", tr_done[s - 2])
        rs = C.RS[:, t:t + 1]
        nmr = C.NMR[:, t:t + 1]
        av = C.op("scalar", lambda e, o=xt, i=C.R[:, t * D:(t + 1) * D], sc=rs, bi=nmr:
                  e.activation(out=o, in_=i, func=AF.Identity, bias=bi, scale=sc), inc="sA")
        C.wait("tensor", "sA", av)
        if s >= 2:
            C.wait("tensor", "sV", cp_done[s - 2])
        for c in range(KC):
            tv = C.op("tensor", lambda e,
                      o=C.psu[:, (s % 2) * 1024 + c * 128:(s % 2) * 1024 + c * 128 + 128],
                      i=xt[:, c * 128:(c + 1) * 128], idm=C.ID:
                      e.transpose(o, i, idm),
                      inc="sT" if c == KC - 1 else None)
        tr_done[s] = tv
        C.wait("vector", "sT", tv)
        for c in range(KC):
            cv = C.op("vector", lambda e,
                      o=XH[:, c * TB + s * 128: c * TB + s * 128 + 128],
                      i=C.psu[:, (s % 2) * 1024 + c * 128:(s % 2) * 1024 + c * 128 + 128]:
                      e.tensor_copy(out=o, in_=i),
                      inc="sV" if c == KC - 1 else None)
        cp_done[s] = cv
    C.xh_ready = C.barrier_vals()
    if b == 0:
        C.tap("xh0", XH)



def emit_moe(C, first):
    m16 = C.a16.mark()
    m32 = C.a32.mark()
    XH = C.a16.alloc(KC * TB)
    H = C.a16.alloc(32 * TB)
    UPR = C.a16.alloc(8 * 1024)
    DNR = C.a16.alloc(8 * 512)
    CWT = C.a16.alloc(NSUB * 128, rows=12)
    C.WG = C.a16.alloc(8 * E)
    C.BDN = C.a16.alloc(D, rows=12)
    io = C.io
    wgv = io["wg"].rearrange("(c p) e -> p c e", p=128)
    C.op("gpsimd", lambda e, o=C.WG.rearrange("p (c e) -> p c e", c=8), i=wgv:
         e.dma_start(out=o, in_=i), inc="sG")
    mwg = C.op("gpsimd", lambda e, o=C.BDN, i=io["bdn"]:
               e.dma_start(out=o, in_=i), inc="sG")
    C.wait("tensor", "sG", mwg)
    ACC = C.a32.alloc(NSUB * D)
    CW = C.a32.alloc(NSUB * E, align=64)
    GL = C.a32.alloc(E, align=64)
    T8 = C.a32.alloc(8, align=64)
    M1N = C.a32.alloc(1, align=4)
    M3C = C.a32.alloc(1, align=4)
    EXL = C.a32.alloc(E, align=64)
    MSK = C.a32.alloc(E, align=64)
    CWU = C.a32.alloc(E, align=64)
    st = {"upr_free": {}, "dnr_free": {}, "up_fill": {}, "dn_fill": {},
          "dn_evac": 0, "gel": {}, "acc_last": {}}

    for b in range(NB):
        emit_xhat(C, b, XH)
        _moe_gate(C, XH, GL, T8, M1N, M3C, EXL, MSK, CWU, CW, CWT, ACC, st)
        if b == 0:
            C.tap("cw0", CW)
            C.tap("acci0", ACC)
        for ee in range(E):
            if first and b <= 1:
                _drain_pending_cast(C)
            _moe_expert(C, ee, XH, H, UPR, DNR, CW, ACC, first, st)
            if b == 0 and ee == 0:
                C.tap("h00", H)
                C.tap("acc00", ACC)
        if b == 0:
            C.tap("acc0", ACC)
        for s in range(NSUB):
            t = b * NSUB + s
            C.wait("vector", "sV", max(st["acc_last"][(s, 0)], st["acc_last"][(s, 1)]))
            C.op("vector", lambda e, o=C.R[:, t * D:(t + 1) * D],
                 i=ACC[:, s * D:(s + 1) * D]:
                 e.scalar_tensor_tensor(out=o, in0=i, scalar=0.5, in1=o,
                                        op0=ALU.mult, op1=ALU.add), inc="sV")
    C.a16.release(m16)
    C.a32.release(m32)


def _moe_gate(C, XH, GL, T8, M1N, M3C, EXL, MSK, CWU, CW, CWT, ACC, st):
    psg = C.psd[:, 0:512]
    psb = C.psd[:, 1024:2048]
    C.wait("tensor", "sV", C.xh_ready["sV"])
    cwt_cp = {}
    for s in range(NSUB):
        mv = None
        for c in range(KC):
            mv = C.op("tensor", lambda e, o=psg[:, 0:E],
                      l=XH[:, c * TB + s * 128: c * TB + s * 128 + 128],
                      r=C.WG[:, c * E:(c + 1) * E], st_=(c == 0), sp=(c == KC - 1):
                      e.matmul(o, l, r, start=st_, stop=sp),
                      inc="sT" if c == KC - 1 else None)
        C.wait("vector", "sT", mv)
        g1 = C.op("vector", lambda e, o=GL, i=psg[:, 0:E]:
                  e.tensor_copy(out=o, in_=i), inc="sV")
        C.wait("vector", "sV", g1)
        g2 = C.op("vector", lambda e, o=T8, i=GL: e.max(out=o, in_=i), inc="sV")
        C.wait("vector", "sV", g2)
        g3 = C.op("vector", lambda e, o=M1N, i=T8[:, 0:1]:
                  e.tensor_scalar(out=o, in0=i, scalar1=-1.0, scalar2=None,
                                  op0=ALU.mult), inc="sV")
        g4 = C.op("vector", lambda e, o=M3C, i=T8[:, 2:3]:
                  e.tensor_copy(out=o, in_=i), inc="sV")
        C.wait("scalar", "sV", g4)
        av = C.op("scalar", lambda e, o=EXL, i=GL, bi=M1N:
                  e.activation(out=o, in_=i, func=AF.Exp, bias=bi), inc="sA")
        C.wait("vector", "sV", g4)
        g5 = C.op("vector", lambda e, o=MSK, i=GL, m=M3C:
                  e.tensor_scalar(out=o, in0=i, scalar1=m, scalar2=None,
                                  op0=ALU.is_ge), inc="sV")
        C.wait("vector", "sA", av)
        C.wait("vector", "sV", g5)
        g6 = C.op("vector", lambda e, o=CWU, a=EXL, b=MSK:
                  e.tensor_mul(o, a, b), inc="sV")
        C.wait("vector", "sV", g6)
        g7 = C.op("vector", lambda e, o=M1N, i=CWU: e.reduce_sum(o, i, AX), inc="sV")
        C.wait("vector", "sV", g7)
        g8 = C.op("vector", lambda e, o=M3C, i=M1N:
                  e.reciprocal(out=o, in_=i), inc="sV")
        C.wait("vector", "sV", g8)
        cw = CW[:, s * E:(s + 1) * E]
        v4 = C.op("vector", lambda e, o=cw, i=CWU, m=M3C:
                  e.tensor_scalar(out=o, in0=i, scalar1=m, scalar2=None,
                                  op0=ALU.mult), inc="sV")
        C.wait("tensor", "sV", v4)
        if s >= 2:
            C.wait("tensor", "sV", cwt_cp[s - 2])
        tv = C.op("tensor", lambda e, o=C.psu[0:12, 0:128], i=cw, idm=C.ID:
                  e.transpose(o, i, idm), inc="sT")
        C.wait("vector", "sT", tv)
        cwt_cp[s] = C.op("vector", lambda e, o=CWT[0:12, s * 128:(s + 1) * 128],
                         i=C.psu[0:12, 0:128]:
                         e.tensor_copy(out=o, in_=i), inc="sV")
        C.wait("tensor", "sV", cwt_cp[s])
        C.op("tensor", lambda e, o=psb[:, 0:512],
             l=CWT[0:12, s * 128:(s + 1) * 128], r=C.BDN[0:12, 0:512]:
             e.matmul(o, l, r, start=True, stop=True))
        b2 = C.op("tensor", lambda e, o=psb[:, 512:1024],
                  l=CWT[0:12, s * 128:(s + 1) * 128], r=C.BDN[0:12, 512:1024]:
                  e.matmul(o, l, r, start=True, stop=True), inc="sT")
        C.wait("vector", "sT", b2)
        iv = C.op("vector", lambda e, o=ACC[:, s * D:(s + 1) * D], i=psb:
                  e.tensor_copy(out=o, in_=i), inc="sV")
        st["acc_last"][(s, 0)] = iv
        st["acc_last"][(s, 1)] = iv
    C.gate_done = C.barrier_vals()


def _moe_expert(C, ee, XH, H, UPR, DNR, CW, ACC, first, st):
    io = C.io
    castv = C.cast_done.get(ee, 0) if first else 0
    st.setdefault("upF", 0)
    st.setdefault("up_cons", {})
    st.setdefault("up_fv", {})
    st.setdefault("dnF", 0)
    st.setdefault("dn_cons", {})
    st.setdefault("dn_fv", {})

    def up_fill(g):
        F = st["upF"]
        st["upF"] += 1
        ring = F % 2
        C.wait("sync", "sT", st["up_cons"].get(F - 2, 0))
        if castv:
            C.wait("sync", "sG", castv)
        dst = UPR[:, ring * 4096:(ring + 1) * 4096]
        src = io["s_wup"][ee, g * 4:(g + 1) * 4]
        st["up_fv"][F] = C.op(
            "sync", lambda e, o=dst.rearrange("p (h f) -> p h f", h=4),
            i=src.rearrange("h p f -> p h f"): e.dma_start(out=o, in_=i), inc="sW")

    up_base = st["upF"]
    up_fill(0)
    up_fill(1)
    C.wait("tensor", "sV", C.gate_done["sV"])
    for grp in range(16):
        hc0 = grp * 2
        par = grp % 2
        F = up_base + hc0 // 4
        C.wait("tensor", "sW", st["up_fv"][F])
        if st["gel"].get(grp - 2):
            C.wait("tensor", "sA", st["gel"][grp - 2])
        tdone = None
        for k in range(2):
            hc = hc0 + k
            ring = F % 2
            base = ring * 4096 + (hc % 4) * 1024
            for kc in range(KC):
                tdone = C.op("tensor", lambda e,
                             o=C.psu[:, par * 1024 + k * 512: par * 1024 + k * 512 + 512],
                             l=UPR[:, base + kc * 128: base + kc * 128 + 128],
                             r=XH[:, kc * TB:(kc + 1) * TB],
                             sa=(kc == 0), sp=(kc == KC - 1):
                             e.matmul(o, l, r, start=sa, stop=sp),
                             inc="sT" if (k == 1 and kc == KC - 1) else None)
        if hc0 % 4 == 2:
            st["up_cons"][F] = tdone
            if hc0 // 4 + 2 < 8:
                up_fill(hc0 // 4 + 2)
        C.wait("scalar", "sT", tdone)
        gv = None
        for k in range(2):
            hc = hc0 + k
            gv = C.op("scalar", lambda e,
                      o=H[:, hc * TB:(hc + 1) * TB],
                      i=C.psu[:, par * 1024 + k * 512: par * 1024 + k * 512 + 512],
                      bi=C.BUP[:, hc * E + ee: hc * E + ee + 1]:
                      e.activation(out=o, in_=i, func=AF.Gelu, bias=bi), inc="sA")
        st["gel"][grp] = gv
    last_up_evac = C.sem["sA"].n

    def dn_fill(g, half):
        F = st["dnF"]
        st["dnF"] += 1
        ring = F % 2
        C.wait("gpsimd", "sT", st["dn_cons"].get(F - 2, 0))
        if castv:
            C.wait("gpsimd", "sG", castv)
        dst = DNR[:, ring * 2048:(ring + 1) * 2048]
        src = io["s_wdn"][ee, g * 512:(g + 1) * 512, half * 512:(half + 1) * 512]
        st["dn_fv"][F] = C.op(
            "gpsimd", lambda e, o=dst.rearrange("p (h f) -> p h f", h=4),
            i=src.rearrange("(h p) f -> p h f", p=128):
            e.dma_start(out=o, in_=i), inc="sG")

    for half in range(2):
        dn_base = st["dnF"]
        dn_fill(0, half)
        dn_fill(1, half)
        if half == 0:
            C.wait("tensor", "sA", last_up_evac)
            C.wait("tensor", "sV", C.sem["sV"].n)
        else:
            C.wait("tensor", "sV", st["dn_evac"])
        tdn = None
        for hc in range(32):
            F = dn_base + hc // 4
            ring = F % 2
            base = ring * 2048 + (hc % 4) * 512
            C.wait("tensor", "sG", st["dn_fv"][F])
            for s in range(NSUB):
                tdn = C.op("tensor", lambda e,
                           o=C.psd[:, s * 512:(s + 1) * 512],
                           l=H[:, hc * TB + s * 128: hc * TB + s * 128 + 128],
                           r=DNR[:, base:base + 512],
                           sa=(hc == 0), sp=(hc == 31):
                           e.matmul(o, l, r, start=sa, stop=sp),
                           inc="sT" if (s == NSUB - 1 and (hc % 4 == 3)) else None)
            if hc % 4 == 3:
                st["dn_cons"][F] = tdn
                if hc // 4 + 2 < 8:
                    dn_fill(hc // 4 + 2, half)
        C.wait("vector", "sT", tdn)
        for s in range(NSUB):
            C.wait("vector", "sV", st["acc_last"].get((s, half), 0))
            st["dn_evac"] = C.op(
                "vector", lambda e,
                o=ACC[:, s * D + half * 512: s * D + half * 512 + 512],
                i=C.psd[:, s * 512:(s + 1) * 512],
                m=CW[:, s * E + ee: s * E + ee + 1]:
                e.scalar_tensor_tensor(out=o, in0=i, scalar=m, in1=o,
                                       op0=ALU.mult, op1=ALU.add), inc="sV")
            st["acc_last"][(s, half)] = st["dn_evac"]



def emit_attn0(C):
    import os
    io = C.io
    m16 = C.a16.mark()
    m32 = C.a32.mark()
    KF = C.a16.alloc(KC * S)
    VT = C.a16.alloc(NT * D)
    XH = C.a16.alloc(KC * TB)
    QB = C.a16.alloc(KC * TB)
    WBX = C.a16.alloc(8192)
    AFB = C.a16.alloc(KC * TB)
    ET = C.a16.alloc(4 * TB)
    SQB = ET[:, 0:1024]
    RQb = C.a16.alloc(TB, rows=1)
    BO0r = C.a16.alloc(D, rows=1)
    NEGM = C.a32.alloc(16, align=64)
    MXK = C.a32.alloc(1, align=4)
    MXQ = C.a32.alloc(1, align=4)
    M2C = C.a32.alloc(1, align=4)
    NRWF = C.a32.alloc(16, rows=1, align=64)
    RQf = C.a32.alloc(TB, rows=1)
    RBs = C.a32.alloc(TB, rows=64)

    C.op("gpsimd", lambda e, o=BO0r, i=io["l0bo"]: e.dma_start(out=o, in_=i), inc="sG")

    def loadw(name, after_sT=0):
        if after_sT:
            C.wait("gpsimd", "sT", after_sT)
        wv = io[name].rearrange("(c p) f -> p c f", p=128)
        dst = WBX.rearrange("p (c f) -> p c f", c=8)
        return C.op("gpsimd", lambda e, o=dst, i=wv: e.dma_start(out=o, in_=i),
                    inc="sG")

    lk = loadw("l0wk")
    C.wait("tensor", "sG", lk)
    ps_rd = {}
    for b2 in range(NB):
        emit_xhat(C, b2, XH)
        C.wait("tensor", "sV", C.xh_ready["sV"])
        stat_last = None
        for c in range(KC):
            par = c % 2
            C.wait("tensor", "sA", ps_rd.get(("k", par), 0))
            tq = None
            for kc in range(KC):
                tq = C.op("tensor", lambda e,
                          o=C.psu[:, par * 512: par * 512 + 512],
                          l=WBX[:, kc * 1024 + c * 128: kc * 1024 + c * 128 + 128],
                          r=XH[:, kc * TB:(kc + 1) * TB],
                          sa=(kc == 0), sp=(kc == KC - 1):
                          e.matmul(o, l, r, start=sa, stop=sp),
                          inc="sT" if kc == KC - 1 else None)
            C.wait("scalar", "sT", tq)
            if c >= 2:
                C.wait("scalar", "sT", stat_last or 0)
            C.op("scalar", lambda e, o=KF[:, c * S + b2 * TB: c * S + b2 * TB + 512],
                 i=C.psu[:, par * 512: par * 512 + 512]:
                 e.activation(out=o, in_=i, func=AF.Identity))
            av = C.op("scalar", lambda e, o=SQB[:, par * 512: par * 512 + 512],
                      i=C.psu[:, par * 512: par * 512 + 512]:
                      e.activation(out=o, in_=i, func=AF.Square), inc="sA")
            ps_rd[("k", par)] = av
            if os.environ.get("ATTN0_LIM") == "K":
                stat_last = None
                continue
            C.wait("tensor", "sA", av)
            stat_last = C.op("tensor", lambda e, o=C.psd[0:16, 0:512],
                             l=C.HSEL[:, c * 16:(c + 1) * 16],
                             r=SQB[:, par * 512: par * 512 + 512],
                             sa=(c == 0), sp=(c == KC - 1):
                             e.matmul(o, l, r, start=sa, stop=sp),
                             inc="sT" if c == KC - 1 else None)
        if os.environ.get("ATTN0_LIM") == "K":
            continue
        C.wait("vector", "sT", stat_last)
        r1 = C.op("vector", lambda e, o=MXQ[0:16, :], i=C.psd[0:16, 0:512]:
                  e.reduce_max(o, i, AX), inc="sV")
        C.wait("vector", "sV", r1)
        if b2 == 0:
            kr = C.op("vector", lambda e, o=MXK[0:16, :], i=MXQ[0:16, :]:
                      e.tensor_copy(out=o, in_=i), inc="sV")
        else:
            kr = C.op("vector", lambda e, o=MXK[0:16, :], a=MXK[0:16, :],
                      b=MXQ[0:16, :]: e.tensor_max(o, a, b), inc="sV")
        C.wait("vector", "sV", kr)
    if os.environ.get("ATTN0_LIM") == "K0":
        C.a16.release(m16)
        C.a32.release(m32)
        return
    lv = loadw("l0wv", after_sT=C.sem["sT"].n)
    C.wait("tensor", "sG", lv)
    for b2 in range(NB):
        emit_xhat(C, b2, XH)
        C.wait("tensor", "sV", C.xh_ready["sV"])
        for sub in range(NSUB):
            t = b2 * NSUB + sub
            par = sub % 2
            C.wait("tensor", "sA", ps_rd.get(("v", par), 0))
            tv2 = None
            for half in range(2):
                for kc in range(KC):
                    tv2 = C.op("tensor", lambda e,
                               o=C.psu[:, par * 1024 + half * 512: par * 1024 + half * 512 + 512],
                               l=XH[:, kc * TB + sub * 128: kc * TB + sub * 128 + 128],
                               r=WBX[:, kc * 1024 + half * 512: kc * 1024 + half * 512 + 512],
                               sa=(kc == 0), sp=(kc == KC - 1):
                               e.matmul(o, l, r, start=sa, stop=sp),
                               inc="sT" if (half == 1 and kc == KC - 1) else None)
            C.wait("scalar", "sT", tv2)
            ps_rd[("v", par)] = C.op(
                "scalar", lambda e, o=VT[:, t * D:(t + 1) * D],
                i=C.psu[:, par * 1024: par * 1024 + 1024]:
                e.activation(out=o, in_=i, func=AF.Identity), inc="sA")

    import os
    if os.environ.get("ATTN0_LIM") in ("A", "K", "KS", "K0"):
        C.a16.release(m16)
        C.a32.release(m32)
        return
    phA = C.barrier_vals()
    av_done = {}
    for qb in range(NB):
        lq = loadw("l0wq", after_sT=C.sem["sT"].n)
        emit_xhat(C, qb, XH)
        C.wait("tensor", "sV", C.xh_ready["sV"])
        C.wait("tensor", "sG", lq)
        stat_last = None
        for c in range(KC):
            par = c % 2
            C.wait("tensor", "sA", ps_rd.get(("q", par), 0))
            tq = None
            for kc in range(KC):
                tq = C.op("tensor", lambda e,
                          o=C.psu[:, par * 512: par * 512 + 512],
                          l=WBX[:, kc * 1024 + c * 128: kc * 1024 + c * 128 + 128],
                          r=XH[:, kc * TB:(kc + 1) * TB],
                          sa=(kc == 0), sp=(kc == KC - 1):
                          e.matmul(o, l, r, start=sa, stop=sp),
                          inc="sT" if kc == KC - 1 else None)
            C.wait("scalar", "sT", tq)
            C.op("scalar", lambda e, o=QB[:, c * TB:(c + 1) * TB],
                 i=C.psu[:, par * 512: par * 512 + 512]:
                 e.activation(out=o, in_=i, func=AF.Identity))
            av = C.op("scalar", lambda e, o=SQB[:, par * 512: par * 512 + 512],
                      i=C.psu[:, par * 512: par * 512 + 512]:
                      e.activation(out=o, in_=i, func=AF.Square), inc="sA")
            ps_rd[("q", par)] = av
            C.wait("tensor", "sA", av)
            stat_last = C.op("tensor", lambda e, o=C.psd[0:16, 0:512],
                             l=C.HSEL[:, c * 16:(c + 1) * 16],
                             r=SQB[:, par * 512: par * 512 + 512],
                             sa=(c == 0), sp=(c == KC - 1):
                             e.matmul(o, l, r, start=sa, stop=sp),
                             inc="sT" if c == KC - 1 else None)
        C.wait("vector", "sT", stat_last)
        r1 = C.op("vector", lambda e, o=MXQ[0:16, :], i=C.psd[0:16, 0:512]:
                  e.reduce_max(o, i, AX), inc="sV")
        C.wait("vector", "sV", r1)
        r2 = C.op("vector", lambda e, o=M2C[0:16, :], a=MXQ[0:16, :], b=MXK[0:16, :]:
                  e.tensor_mul(o, a, b), inc="sV")
        C.wait("scalar", "sV", r2)
        a1 = C.op("scalar", lambda e, o=M2C[0:16, :], i=M2C[0:16, :]:
                  e.activation(out=o, in_=i, func=AF.Ln), inc="sA")
        C.wait("scalar", "sA", a1)
        a2 = C.op("scalar", lambda e, o=M2C[0:16, :], i=M2C[0:16, :],
                  sc=C.KON[0:16, 4:5], bi=C.KON[0:16, 5:6]:
                  e.activation(out=o, in_=i, func=AF.Exp, scale=sc, bias=bi),
                  inc="sA")
        C.wait("vector", "sA", a2)
        r3 = C.op("vector", lambda e, o=M2C[0:16, :], i=M2C[0:16, :]:
                  e.tensor_scalar(out=o, in0=i, scalar1=-1.0, scalar2=None,
                                  op0=ALU.mult), inc="sV")
        C.wait("tensor", "sV", r3)
        t1 = C.op("tensor", lambda e, o=C.psd[0:1, 512:528], i=M2C[0:16, :],
                  idm=C.ID[0:16, 0:16]: e.transpose(o, i, idm), inc="sT")
        C.wait("vector", "sT", t1)
        r4 = C.op("vector", lambda e, o=NRWF, i=C.psd[0:1, 512:528]:
                  e.tensor_copy(out=o, in_=i), inc="sV")
        C.wait("vector", "sV", r4)
        r5 = C.op("vector", lambda e, o=RQb[0:1, 0:16], i=NRWF:
                  e.tensor_copy(out=o, in_=i), inc="sV")
        C.wait("tensor", "sV", r5)
        t2 = C.op("tensor", lambda e, o=C.psd[:, 512:1024], l=C.ONES_R,
                  r=RQb[0:1, 0:16]:
                  e.matmul(o[:, 0:16], l, r, start=True, stop=True), inc="sT")
        C.wait("vector", "sT", t2)
        r6 = C.op("vector", lambda e, o=NEGM, i=C.psd[:, 512:528]:
                  e.tensor_copy(out=o, in_=i), inc="sV")
        C.wait("tensor", "sV", r6)
        C.wait("scalar", "sV", r6)
        for h in range(NH):
            rb = 64 * (h % 2)
            ch = h // 2
            exp_done = {}
            avl = dnl = None
            for jp in range(8):
                par = jp % 2
                if jp >= 2:
                    C.wait("tensor", "sA", exp_done[jp - 2])
                for jj in range(2):
                    j = 2 * jp + jj
                    C.op("tensor", lambda e,
                         o=C.psu[:, par * 1024 + jj * 512: par * 1024 + jj * 512 + 512],
                         l=KF[rb:rb + 64, ch * S + j * 128: ch * S + j * 128 + 128],
                         r=QB[rb:rb + 64, ch * TB: ch * TB + 512],
                         sa=True, sp=True:
                         e.matmul(o, l, r, start=sa, stop=sp),
                         inc="sT" if jj == 1 else None)
                sc_done = C.sem["sT"].n
                C.wait("scalar", "sT", sc_done)
                exp_done[jp] = C.op(
                    "scalar", lambda e, o=ET[:, par * 1024: par * 1024 + 1024],
                    i=C.psu[:, par * 1024: par * 1024 + 1024],
                    sc=C.KON[:, 0:1], bi=NEGM[:, h:h + 1]:
                    e.activation(out=o, in_=i, func=AF.Exp, scale=sc, bias=bi),
                    inc="sA")
                C.wait("tensor", "sA", exp_done[jp])
                for jj in range(2):
                    j = 2 * jp + jj
                    avl = C.op("tensor", lambda e, o=C.psd[0:64, 0:512],
                               l=VT[:, j * D + h * 64: j * D + h * 64 + 64],
                               r=ET[:, par * 1024 + jj * 512: par * 1024 + jj * 512 + 512],
                               sa=(j == 0), sp=(j == NT - 1):
                               e.matmul(o, l, r, start=sa, stop=sp))
                    dnl = C.op("tensor", lambda e, o=C.psd[0:1, 512:1024],
                               l=C.ONES_C,
                               r=ET[:, par * 1024 + jj * 512: par * 1024 + jj * 512 + 512],
                               sa=(j == 0), sp=(j == NT - 1):
                               e.matmul(o, l, r, start=sa, stop=sp),
                               inc="sT" if (jp == 7 and jj == 1) else None)
                av_done[(h, jp)] = C.sem["sT"].n
            dn_done = C.sem["sT"].n
            C.wait("vector", "sT", dn_done)
            n1 = C.op("vector", lambda e, o=RQf, i=C.psd[0:1, 512:1024]:
                      e.reciprocal(out=o, in_=i), inc="sV")
            C.wait("vector", "sV", n1)
            n2 = C.op("vector", lambda e, o=RQb[0:1, :], i=RQf:
                      e.tensor_copy(out=o, in_=i), inc="sV")
            C.wait("tensor", "sV", n2)
            t3 = C.op("tensor", lambda e, o=C.psd[0:64, 1024:1536],
                      l=C.ONES_R[0:1, 0:64], r=RQb[0:1, :]:
                      e.matmul(o, l, r, start=True, stop=True), inc="sT")
            C.wait("vector", "sT", t3)
            n3 = C.op("vector", lambda e, o=RBs, i=C.psd[0:64, 1024:1536]:
                      e.tensor_copy(out=o, in_=i), inc="sV")
            C.wait("vector", "sV", n3)
            n4 = C.op("vector", lambda e,
                      o=AFB[rb:rb + 64, ch * TB: ch * TB + 512],
                      a=C.psd[0:64, 0:512], b=RBs:
                      e.tensor_mul(o, a, b), inc="sV")
            C.wait("tensor", "sV", n4)
        if os.environ.get("ATTN0_LIM") == "B1":
            break
        lo = loadw("l0wo", after_sT=C.sem["sT"].n)
        C.wait("tensor", "sG", lo)
        C.wait("tensor", "sV", C.sem["sV"].n)
        prev_o = 0
        for sub in range(NSUB):
            t = qb * NSUB + sub
            C.wait("tensor", "sV", prev_o)
            to = None
            for half in range(2):
                for kc in range(KC):
                    C.op("tensor", lambda e,
                         o=C.psd[:, 1024 + half * 512: 1024 + half * 512 + 512],
                         l=AFB[:, kc * TB + sub * 128: kc * TB + sub * 128 + 128],
                         r=WBX[:, kc * 1024 + half * 512: kc * 1024 + half * 512 + 512],
                         sa=(kc == 0), sp=False:
                         e.matmul(o, l, r, start=sa, stop=sp))
                to = C.op("tensor", lambda e,
                          o=C.psd[:, 1024 + half * 512: 1024 + half * 512 + 512],
                          l=C.ONES_R, r=BO0r[0:1, half * 512: half * 512 + 512]:
                          e.matmul(o, l, r, start=False, stop=True),
                          inc="sT" if half == 1 else None)
            C.wait("vector", "sT", to)
            prev_o = C.op("vector", lambda e, o=C.R[:, t * D:(t + 1) * D],
                          i=C.psd[:, 1024:2048]:
                          e.tensor_add(o, i, o), inc="sV")
    C.a16.release(m16)
    C.a32.release(m32)


def emit_l1ffn(C, first):
    io = C.io
    m16 = C.a16.mark()
    m32 = C.a32.mark()
    XH = C.a16.alloc(KC * TB)
    H = C.a16.alloc(32 * TB)
    G = C.a16.alloc(32 * TB)
    RNG = C.a16.alloc(8192)
    W12R = G
    T1 = C.a32.alloc(2 * TB)
    T2 = C.a32.alloc(2 * TB)
    O3 = C.a32.alloc(2 * TB)
    if first:
        C.wait("sync", "sG", C.l1_cast_done)
        C.wait("gpsimd", "sG", C.l1_cast_done)
    st = {"F": {}, "fv": {}, "cons": {}}

    def fill(stream, qn, src_ap, dst_ap):
        F = st["F"].get(stream, 0)
        st["F"][stream] = F + 1
        C.wait(qn, "sT", st["cons"].get((stream, F - 2), 0))
        st["fv"][(stream, F)] = C.op(qn, lambda e, o=dst_ap, i=src_ap:
                                     e.dma_start(out=o, in_=i),
                                     inc="sW" if qn == "sync" else "sG")
        return F

    def w12_fill(m, g):
        srcv = io["s_w1t" if m == 0 else "s_w2t"][g * 4:(g + 1) * 4]
        dst = W12R[:, m * 8192 + (g % 2) * 4096: m * 8192 + (g % 2) * 4096 + 4096]
        return fill("w%d" % m, "sync" if m == 0 else "gpsimd",
                    srcv.rearrange("h p f -> p h f"),
                    dst.rearrange("p (h f) -> p h f", h=4))

    def big_fill(stream, srckey, g):
        return fill(stream, "sync" if g % 2 == 0 else "gpsimd",
                    io[srckey][g], RNG[:, (g % 2) * 4096:(g % 2) * 4096 + 4096])

    for b in range(NB):
        emit_xhat(C, b, XH)
        C.wait("tensor", "sV", C.xh_ready["sV"])
        fmap = {}
        for m in (0, 1):
            for g in range(2):
                fmap[(m, g)] = w12_fill(m, g)
        ev = {}
        for oc in range(32):
            g = oc // 4
            par = oc % 2
            for m in (0, 1):
                C.wait("tensor", "sW" if m == 0 else "sG",
                       st["fv"][("w%d" % m, fmap[(m, g)])])
            if ev.get(oc - 2):
                C.wait("tensor", "sV", ev[oc - 2])
            base0 = (g % 2) * 4096 + (oc % 4) * 1024
            t1 = None
            for m in (0, 1):
                for kc in range(KC):
                    t1 = C.op("tensor", lambda e,
                              o=C.psu[:, m * 1024 + par * 512: m * 1024 + par * 512 + 512],
                              l=W12R[:, m * 8192 + base0 + kc * 128: m * 8192 + base0 + kc * 128 + 128],
                              r=XH[:, kc * TB:(kc + 1) * TB],
                              sa=(kc == 0), sp=(kc == KC - 1):
                              e.matmul(o, l, r, start=sa, stop=sp),
                              inc="sT" if (m == 1 and kc == KC - 1) else None)
            if oc % 4 == 3:
                st["cons"][("w0", fmap[(0, g)])] = t1
                st["cons"][("w1", fmap[(1, g)])] = t1
                if g + 2 < 8:
                    fmap[(0, g + 2)] = w12_fill(0, g + 2)
                    fmap[(1, g + 2)] = w12_fill(1, g + 2)
            C.wait("scalar", "sT", t1)
            if ev.get(oc - 2):
                C.wait("scalar", "sV", ev[oc - 2])
            a1 = C.op("scalar", lambda e, o=T1[:, par * TB:(par + 1) * TB],
                      i=C.psu[:, par * 512: par * 512 + 512], sc=C.S1[:, oc:oc + 1]:
                      e.activation(out=o, in_=i, func=AF.Silu, scale=sc), inc="sA")
            a2 = C.op("scalar", lambda e, o=T2[:, par * TB:(par + 1) * TB],
                      i=C.psu[:, 1024 + par * 512: 1024 + par * 512 + 512],
                      sc=C.S2[:, oc:oc + 1]:
                      e.activation(out=o, in_=i, func=AF.Identity, scale=sc), inc="sA")
            C.wait("vector", "sA", a2)
            ev[oc] = C.op("vector", lambda e, o=H[:, oc * TB:(oc + 1) * TB],
                          a=T1[:, par * TB:(par + 1) * TB],
                          b=T2[:, par * TB:(par + 1) * TB]:
                          e.tensor_mul(o, a, b), inc="sV")
        for phase, srckey, IN, OUT, SCL in (("h1", "s_h1t", H, G, C.H1S),
                                            ("h2", "s_h2t", G, H, C.H2S)):
            fills = {}
            for g in range(2):
                fills[g] = big_fill(phase + str(b), srckey, g)
            inb = C.barrier_vals()
            C.wait("tensor", "sV", inb["sV"])
            C.wait("tensor", "sA", inb["sA"])
            evp = {}
            for oc in range(32):
                par = oc % 2
                C.wait("tensor", "sW" if oc % 2 == 0 else "sG",
                       st["fv"][(phase + str(b), fills[oc])])
                if evp.get(oc - 2):
                    C.wait("tensor", "sA", evp[oc - 2])
                t1 = None
                for kc in range(32):
                    t1 = C.op("tensor", lambda e,
                              o=C.psu[:, par * 512: par * 512 + 512],
                              l=RNG[:, (oc % 2) * 4096 + kc * 128: (oc % 2) * 4096 + kc * 128 + 128],
                              r=IN[:, kc * TB:(kc + 1) * TB],
                              sa=(kc == 0), sp=(kc == 31):
                              e.matmul(o, l, r, start=sa, stop=sp),
                              inc="sT" if kc == 31 else None)
                st["cons"][(phase + str(b), fills[oc])] = t1
                if oc + 2 < 32:
                    fills[oc + 2] = big_fill(phase + str(b), srckey, oc + 2)
                C.wait("scalar", "sT", t1)
                evp[oc] = C.op("scalar", lambda e, o=OUT[:, oc * TB:(oc + 1) * TB],
                               i=C.psu[:, par * 512: par * 512 + 512],
                               sc=SCL[:, oc:oc + 1]:
                               e.activation(out=o, in_=i, func=AF.Gelu, scale=sc),
                               inc="sA")
        for m in (0, 1):
            for g in (6, 7):
                st["cons"][("w%d" % m, b * 8 + g)] = C.sem["sT"].n
        fills = {}
        for g in range(2):
            fills[g] = big_fill("w3" + str(b), "s_w3t", g)
        C.wait("tensor", "sA", C.sem["sA"].n)
        tr_prev = {}
        for dc in range(8):
            par = dc % 2
            C.wait("tensor", "sW" if dc % 2 == 0 else "sG",
                   st["fv"][("w3" + str(b), fills[dc])])
            if tr_prev.get(dc - 2):
                C.wait("tensor", "sA", tr_prev[dc - 2][2])
            t1 = None
            for kc in range(32):
                t1 = C.op("tensor", lambda e,
                          o=C.psu[:, par * 512: par * 512 + 512],
                          l=RNG[:, (dc % 2) * 4096 + kc * 128: (dc % 2) * 4096 + kc * 128 + 128],
                          r=H[:, kc * TB:(kc + 1) * TB],
                          sa=(kc == 0), sp=(kc == 31):
                          e.matmul(o, l, r, start=sa, stop=sp),
                          inc="sT" if kc == 31 else None)
            st["cons"][("w3" + str(b), fills[dc])] = t1
            if dc + 2 < 8:
                fills[dc + 2] = big_fill("w3" + str(b), "s_w3t", dc + 2)
            C.wait("scalar", "sT", t1)
            if tr_prev.get(dc - 2):
                C.wait("scalar", "sT", tr_prev[dc - 2][0])
            a1 = C.op("scalar", lambda e, o=O3[:, par * TB:(par + 1) * TB],
                      i=C.psu[:, par * 512: par * 512 + 512], sc=C.W3S[:, dc:dc + 1]:
                      e.activation(out=o, in_=i, func=AF.Identity, scale=sc), inc="sA")
            C.wait("tensor", "sA", a1)
            if tr_prev.get(dc - 2):
                C.wait("tensor", "sV", tr_prev[dc - 2][1])
            tvl = None
            for sub in range(NSUB):
                tvl = C.op("tensor", lambda e,
                           o=C.psd[:, par * 1024 + sub * 128: par * 1024 + sub * 128 + 128],
                           i=O3[:, par * TB + sub * 128: par * TB + sub * 128 + 128],
                           idm=C.ID: e.transpose(o, i, idm),
                           inc="sT" if sub == NSUB - 1 else None)
            C.wait("vector", "sT", tvl)
            vvl = None
            for sub in range(NSUB):
                t = b * NSUB + sub
                vvl = C.op("vector", lambda e,
                           o=C.R[:, t * D + dc * 128: t * D + dc * 128 + 128],
                           i=C.psd[:, par * 1024 + sub * 128: par * 1024 + sub * 128 + 128]:
                           e.tensor_add(o, i, o), inc="sV" if sub == NSUB - 1 else None)
            tr_prev[dc] = (tvl, vvl, a1)
    C.a16.release(m16)
    C.a32.release(m32)


def emit_attn1(C):
    io = C.io
    m16 = C.a16.mark()
    m32 = C.a32.mark()
    XH = C.a16.alloc(KC * TB)
    KTs = C.a16.alloc(D)
    VTs = C.a16.alloc(D)
    KV2 = C.a16.alloc(D)
    QF2 = C.a16.alloc(KC * TB)
    GT = C.a16.alloc(KC * TB)
    AF2 = C.a16.alloc(KC * TB)
    WB = C.a16.alloc(3 * 8192)
    C.BKr = C.a16.alloc(D, rows=1)
    C.BVr = C.a16.alloc(D, rows=1)
    C.BO1r = C.a16.alloc(D, rows=1)
    for name, dst in (("l1bk", C.BKr), ("l1bv", C.BVr), ("l1bo", C.BO1r)):
        C.op("gpsimd", lambda e, o=dst, i=io[name]: e.dma_start(out=o, in_=i), inc="sG")
    E1 = C.a32.alloc(D)
    E3 = C.a32.alloc(D)

    def loadw(name, pos):
        wv = io[name].rearrange("(c p) f -> p c f", p=128)
        dst = WB[:, pos * 8192:(pos + 1) * 8192].rearrange("p (c f) -> p c f", c=8)
        return C.op("gpsimd", lambda e, o=dst, i=wv: e.dma_start(out=o, in_=i),
                    inc="sG")

    lk = loadw("l1wk", 0)
    lv = loadw("l1wv", 1)
    C.wait("tensor", "sG", lv)
    psum_rd = {"k": 0, "v": 0}
    kv_last = None
    for b2 in range(NB):
        emit_xhat(C, b2, XH)
        C.wait("tensor", "sV", C.xh_ready["sV"])
        for s in range(NSUB):
            j = b2 * NSUB + s
            C.wait("tensor", "sV", psum_rd["k"])
            C.wait("tensor", "sA", psum_rd["v"])
            tk = tv = None
            for which, wpos, brow, pbase in (("k", 0, C.BKr, 0), ("v", 1, C.BVr, 1024)):
                for half in range(2):
                    for kc in range(KC):
                        C.op("tensor", lambda e,
                             o=C.psu[:, pbase + half * 512: pbase + half * 512 + 512],
                             l=XH[:, kc * TB + s * 128: kc * TB + s * 128 + 128],
                             r=WB[:, wpos * 8192 + kc * 1024 + half * 512:
                                   wpos * 8192 + kc * 1024 + half * 512 + 512],
                             sa=(kc == 0), sp=False:
                             e.matmul(o, l, r, start=sa, stop=sp))
                    last = C.op("tensor", lambda e,
                                o=C.psu[:, pbase + half * 512: pbase + half * 512 + 512],
                                l=C.ONES_R, r=brow[0:1, half * 512: half * 512 + 512]:
                                e.matmul(o, l, r, start=False, stop=True),
                                inc="sT" if half == 1 else None)
                if which == "k":
                    tk = last
                else:
                    tv = last
            C.wait("vector", "sT", tk)
            e1 = C.op("vector", lambda e, o=E1, i=C.psu[:, 0:1024]:
                      e.tensor_scalar(out=o, in0=i, scalar1=0.0, scalar2=None,
                                      op0=ALU.min), inc="sV")
            C.wait("scalar", "sV", e1)
            a1 = C.op("scalar", lambda e, o=E1, i=E1:
                      e.activation(out=o, in_=i, func=AF.Exp), inc="sA")
            e3 = C.op("vector", lambda e, o=E3, i=C.psu[:, 0:1024]:
                      e.tensor_scalar(out=o, in0=i, scalar1=0.0, scalar2=None,
                                      op0=ALU.max), inc="sV")
            psum_rd["k"] = e3
            C.wait("vector", "sA", a1)
            C.wait("vector", "sV", e3)
            kts = C.op("vector", lambda e, o=KTs, a=E1, b=E3:
                       e.tensor_add(o, a, b), inc="sV")
            C.wait("scalar", "sT", tv)
            vts = C.op("scalar", lambda e, o=VTs, i=C.psu[:, 1024:2048]:
                       e.activation(out=o, in_=i, func=AF.Identity), inc="sA")
            psum_rd["v"] = vts
            C.wait("tensor", "sV", kts)
            C.wait("tensor", "sA", vts)
            for h in range(NH):
                kv_last = C.op("tensor", lambda e,
                               o=C.psd[0:64, h * 64:(h + 1) * 64],
                               l=KTs[:, h * 64:(h + 1) * 64],
                               r=VTs[:, h * 64:(h + 1) * 64],
                               sa=(j == 0 and h % 8 == 0), sp=(j == NT - 1):
                               e.matmul(o, l, r, start=sa, stop=sp),
                               inc="sT" if h == NH - 1 else None)
    C.wait("vector", "sT", kv_last)
    kva = C.op("vector", lambda e, o=KV2[0:64, :], i=C.psd[0:64, 0:1024]:
               e.tensor_copy(out=o, in_=i), inc="sV")
    kvb = C.op("vector", lambda e, o=KV2[64:128, :], i=C.psd[0:64, 0:1024]:
               e.tensor_copy(out=o, in_=i), inc="sV")

    ph1 = C.barrier_vals()
    C.wait("gpsimd", "sT", ph1["sT"])
    lq = loadw("l1wq", 0)
    lg = loadw("l1wg", 1)
    lo = loadw("l1wo", 2)
    C.wait("tensor", "sG", lo)
    C.wait("tensor", "sV", kvb)
    psum_rd2 = {}
    af_ev = {}
    for b2 in range(NB):
        emit_xhat(C, b2, XH)
        C.wait("tensor", "sV", C.xh_ready["sV"])
        qf_done = None
        for c in range(KC):
            par = c % 2
            C.wait("tensor", "sV", psum_rd2.get(("q", par), 0))
            tq = None
            for kc in range(KC):
                tq = C.op("tensor", lambda e,
                          o=C.psu[:, par * 512: par * 512 + 512],
                          l=WB[:, kc * 1024 + c * 128: kc * 1024 + c * 128 + 128],
                          r=XH[:, kc * TB:(kc + 1) * TB],
                          sa=(kc == 0), sp=(kc == KC - 1):
                          e.matmul(o, l, r, start=sa, stop=sp),
                          inc="sT" if kc == KC - 1 else None)
            C.wait("vector", "sT", tq)
            e1 = C.op("vector", lambda e, o=E1[:, 0:512],
                      i=C.psu[:, par * 512: par * 512 + 512], bi=C.BQT[:, c:c + 1]:
                      e.tensor_scalar(out=o, in0=i, scalar1=bi, scalar2=0.0,
                                      op0=ALU.add, op1=ALU.min), inc="sV")
            C.wait("scalar", "sV", e1)
            a1 = C.op("scalar", lambda e, o=E1[:, 0:512], i=E1[:, 0:512]:
                      e.activation(out=o, in_=i, func=AF.Exp), inc="sA")
            e3 = C.op("vector", lambda e, o=E3[:, 0:512],
                      i=C.psu[:, par * 512: par * 512 + 512], bi=C.BQT[:, c:c + 1]:
                      e.tensor_scalar(out=o, in0=i, scalar1=bi, scalar2=0.0,
                                      op0=ALU.add, op1=ALU.max), inc="sV")
            psum_rd2[("q", par)] = e3
            C.wait("vector", "sA", a1)
            C.wait("vector", "sV", e3)
            qf_done = C.op("vector", lambda e, o=QF2[:, c * TB:(c + 1) * TB],
                           a=E1[:, 0:512], b=E3[:, 0:512]:
                           e.tensor_add(o, a, b), inc="sV")
        C.wait("tensor", "sV", qf_done)
        for h in range(NH):
            quad = h % 2
            C.wait("tensor", "sA", af_ev.get(quad, 0))
            ta = C.op("tensor", lambda e,
                      o=C.psu[0:64, 1024 + quad * 512: 1024 + quad * 512 + 512],
                      l=KV2[64 * (h % 2):64 * (h % 2) + 64, h * 64:(h + 1) * 64],
                      r=QF2[64 * (h % 2):64 * (h % 2) + 64,
                            (h // 2) * TB:(h // 2) * TB + 512]:
                      e.matmul(o, l, r, start=True, stop=True), inc="sT")
            C.wait("scalar", "sT", ta)
            af_ev[quad] = C.op("scalar", lambda e,
                               o=AF2[64 * (h % 2):64 * (h % 2) + 64,
                                     (h // 2) * TB:(h // 2) * TB + 512],
                               i=C.psu[0:64, 1024 + quad * 512: 1024 + quad * 512 + 512]:
                               e.activation(out=o, in_=i, func=AF.Identity),
                               inc="sA")
        gt_ev = {}
        for c in range(KC):
            par = c % 2
            C.wait("tensor", "sA", gt_ev.get(c - 2, 0))
            tg = None
            for kc in range(KC):
                tg = C.op("tensor", lambda e,
                          o=C.psu[:, par * 512: par * 512 + 512],
                          l=WB[:, 8192 + kc * 1024 + c * 128: 8192 + kc * 1024 + c * 128 + 128],
                          r=XH[:, kc * TB:(kc + 1) * TB],
                          sa=(kc == 0), sp=(kc == KC - 1):
                          e.matmul(o, l, r, start=sa, stop=sp),
                          inc="sT" if kc == KC - 1 else None)
            C.wait("scalar", "sT", tg)
            gt_ev[c] = C.op("scalar", lambda e, o=GT[:, c * TB:(c + 1) * TB],
                            i=C.psu[:, par * 512: par * 512 + 512],
                            bi=C.BGT[:, c:c + 1]:
                            e.activation(out=o, in_=i, func=AF.Sigmoid, bias=bi),
                            inc="sA")
        zv = None
        C.wait("vector", "sA", max(gt_ev[KC - 1], max(af_ev.values())))
        for c in range(KC):
            zv = C.op("vector", lambda e, o=GT[:, c * TB:(c + 1) * TB],
                      a=GT[:, c * TB:(c + 1) * TB], b=AF2[:, c * TB:(c + 1) * TB]:
                      e.tensor_mul(o, a, b), inc="sV" if c == KC - 1 else None)
        C.wait("tensor", "sV", zv)
        prev_o = 0
        for sub in range(NSUB):
            t = b2 * NSUB + sub
            C.wait("tensor", "sV", prev_o)
            to = None
            for half in range(2):
                for kc in range(KC):
                    C.op("tensor", lambda e,
                         o=C.psd[:, 1024 + half * 512: 1024 + half * 512 + 512],
                         l=GT[:, kc * TB + sub * 128: kc * TB + sub * 128 + 128],
                         r=WB[:, 16384 + kc * 1024 + half * 512:
                               16384 + kc * 1024 + half * 512 + 512],
                         sa=(kc == 0), sp=False:
                         e.matmul(o, l, r, start=sa, stop=sp))
                to = C.op("tensor", lambda e,
                          o=C.psd[:, 1024 + half * 512: 1024 + half * 512 + 512],
                          l=C.ONES_R, r=C.BO1r[0:1, half * 512: half * 512 + 512]:
                          e.matmul(o, l, r, start=False, stop=True),
                          inc="sT" if half == 1 else None)
            C.wait("vector", "sT", to)
            prev_o = C.op("vector", lambda e, o=C.R[:, t * D:(t + 1) * D],
                          i=C.psd[:, 1024:2048]:
                          e.tensor_add(o, i, o), inc="sV")
    C.a16.release(m16)
    C.a32.release(m32)


def emit_epilogue(C):
    io = C.io
    fin = C.barrier_vals()
    C.wait_all("sync", fin)
    ov = io["out"].rearrange("(t p) d -> p t d", p=128)
    rv = C.R.rearrange("p (t d) -> p t d", t=NT)
    C.op("sync", lambda e, o=ov, i=rv: e.dma_start(out=o, in_=i), inc="sO")
    C.wait("sync", "sO", C.sem["sO"].n)



def kernel(**inputs):
    ins = host_layout(inputs)
    x = ins.pop("x")
    nc = build_nc(FULL_CHAIN)
    in_maps = []
    for c in range(8):
        m = dict(ins)
        m["x"] = np.ascontiguousarray(x[c])
        in_maps.append(m)
    res = run_bass_kernel_spmd(nc, in_maps, core_ids=list(range(8)))
    out = np.stack([res.results[c]["out"] for c in range(8)], axis=0)
    return out.astype(np.float32)
